# version 08 (final)
# baseline (speedup 1.0000x reference)
import math
from contextlib import ExitStack

import numpy as np
import ml_dtypes

import concourse.bass as bass
import concourse.mybir as mybir
from concourse.bass_utils import run_bass_kernel_spmd

F32 = mybir.dt.float32
BF16 = mybir.dt.bfloat16
I32 = mybir.dt.int32
FP8 = mybir.dt.float8e4
Act = mybir.ActivationFunctionType
Alu = mybir.AluOpType
AX = mybir.AxisListType.X

BF = ml_dtypes.bfloat16
E4 = ml_dtypes.float8_e4m3fn

B, S, D, H, HD = 4, 2048, 2048, 16, 128
TOK = 1024
TF = D // 128
I0, I1 = 5504, 8192
NT0, NT1 = I0 // 128, I1 // 128
SCALE = 1.0 / math.sqrt(HD)
ESHIFT = -12.0
NEG = -1.0e30


class Prog:
    def __init__(self):
        self.nc = bass.Bass("TRN2", target_bir_lowering=False, debug=False,
                            detect_race_conditions=False, num_devices=8)
        self.stk = ExitStack()
        self.sems = {}
        self.cnt = {}
        self._waited = {}
        self.block = None

    def din(self, name, shape, dt=F32):
        return self.nc.dram_tensor(name, list(shape), dt, kind="ExternalInput").ap()

    def dout(self, name, shape, dt=F32):
        return self.nc.dram_tensor(name, list(shape), dt, kind="ExternalOutput").ap()

    def sb(self, name, shape, dt=F32):
        return self.stk.enter_context(self.nc.sbuf_tensor(name, list(shape), dt))[:, :]

    def ps(self, name, shape, dt=F32):
        return self.stk.enter_context(self.nc.psum_tensor(name, list(shape), dt))[:, :]

    def sem(self, n):
        if n not in self.sems:
            self.sems[n] = self.stk.enter_context(self.nc.semaphore(n))
            self.cnt[n] = 0
        return self.sems[n]

    def start(self):
        for n in ("dma", "dmo", "pe", "ac", "ve"):
            self.sem(n)
        self.block = self.stk.enter_context(self.nc.Block())
        cbank = self.sb("constbank", (128, 2))
        self.nc.const_aps.register(F32, ESHIFT, cbank[:, 0:1])
        self.nc.const_aps.register(F32, SCALE, cbank[:, 1:2])

        def v_const(e, w, inc, bump):
            e.memset(cbank[:, 0:1], ESHIFT)
            e.memset(cbank[:, 1:2], SCALE)
            bump("ve")
        self.emit("vector", v_const)
        self.const_ve = self.cnt["ve"]

    def emit(self, engname, fn):
        sec = getattr(self.block, engname)

        def wrap(e):
            state = {"last": None}

            class Proxy:
                def __getattr__(_self, name):
                    attr = getattr(e, name)
                    if not callable(attr):
                        return attr

                    def call(*a, **kw):
                        r = attr(*a, **kw)
                        if name not in ("wait_ge", "sem_inc") and hasattr(r, "then_inc"):
                            state["last"] = r
                        return r
                    return call

            ep = Proxy()

            def w(n, val):
                if val <= 0:
                    return
                key = (engname, n)
                if self._waited.get(key, -1) < val:
                    e.wait_ge(self.sems[n], val)
                    self._waited[key] = val

            def inc(inst, n, k=16):
                inst.then_inc(self.sems[n], k)
                self.cnt[n] += k

            def bump(n, k=1):
                if state["last"] is not None:
                    state["last"].then_inc(self.sems[n], k)
                    state["last"] = None
                else:
                    e.sem_inc(self.sems[n], k)
                self.cnt[n] += k

            fn(ep, w, inc, bump)

        sec(wrap)

    def finish(self):
        total = self.cnt["dmo"]

        def g(e, w, inc, bump):
            w("dmo", total)

        self.emit("gpsimd", g)
        self.stk.close()
        return self.nc


def run(prog_builder, in_maps):
    nc = prog_builder()
    res = run_bass_kernel_spmd(nc, in_maps, core_ids=list(range(8)))
    return res.results


def ln_mod_fm(p, x_fm, h_fm, c1_cols, s_cols, psS, psQ, psR, psM, xsqa, xsq, rowbuf,
              sonesc, sonesr, xdeps, ln_steps=5):
    xsqa = xsqa[:, :]
    xsq = xsq[:, :]
    for tc in range(2):
        sq_mm_done = {}
        for ft in range(TF):
            def a_sq(e, w, inc, bump, ft=ft, tc=tc):
                for sn, val in xdeps.items():
                    w(sn, val)
                w("pe", sq_mm_done.get(ft - 1, 0))
                e.activation(out=xsqa,
                             in_=x_fm[:, ft * TOK + tc * 512: ft * TOK + tc * 512 + 512],
                             func=Act.Square)
                bump("ac")
            p.emit("scalar", a_sq)
            ac_now = p.cnt["ac"]

            def t_mm(e, w, inc, bump, ft=ft, tc=tc, ac_now=ac_now):
                for sn, val in xdeps.items():
                    w(sn, val)
                w("ac", ac_now)
                if ft == 0:
                    w("ve", p.cnt["ve"])
                e.matmul(psS, lhsT=sonesc[:, :],
                         rhs=x_fm[:, ft * TOK + tc * 512: ft * TOK + tc * 512 + 512],
                         start=(ft == 0), stop=(ft == TF - 1))
                e.matmul(psQ, lhsT=sonesc[:, :], rhs=xsqa[:, :], start=(ft == 0), stop=(ft == TF - 1))
                bump("pe")
            p.emit("tensor", t_mm)
            sq_mm_done[ft] = p.cnt["pe"]
        if ln_steps < 1:
            continue

        def v_stats(e, w, inc, bump, tc=tc):
            w("pe", p.cnt["pe"])
            m = rowbuf[0:1, 0:512]
            ex2 = rowbuf[0:1, 512:1024]
            e.tensor_scalar(out=m, in0=psS, scalar1=1.0 / D, scalar2=None, op0=Alu.mult)
            e.tensor_scalar(out=ex2, in0=psQ, scalar1=1.0 / D, scalar2=None, op0=Alu.mult)
            e.tensor_tensor(out=psQ, in0=m, in1=m, op=Alu.mult)
            e.tensor_tensor(out=psQ, in0=ex2, in1=psQ, op=Alu.subtract)
            e.tensor_scalar(out=psQ, in0=psQ, scalar1=1e-5, scalar2=None, op0=Alu.add)
            bump("ve")
        p.emit("vector", v_stats)
        if ln_steps < 2:
            continue

        def a_rstd(e, w, inc, bump, tc=tc):
            w("ve", p.cnt["ve"])
            e.activation(out=rowbuf[0:1, 512:1024], in_=psQ, func=Act.Sqrt)
            bump("ac")
        p.emit("scalar", a_rstd)
        if ln_steps < 3:
            continue

        def v_mb(e, w, inc, bump, tc=tc):
            w("ac", p.cnt["ac"])
            e.reciprocal(out=rowbuf[0:1, 512:1024], in_=rowbuf[0:1, 512:1024])
            e.tensor_tensor(out=rowbuf[0:1, 0:512], in0=rowbuf[0:1, 0:512],
                            in1=rowbuf[0:1, 512:1024], op=Alu.mult)
            e.tensor_scalar(out=rowbuf[0:1, 0:512], in0=rowbuf[0:1, 0:512],
                            scalar1=-1.0, scalar2=None, op0=Alu.mult)
            bump("ve")
        p.emit("vector", v_mb)
        if ln_steps < 4:
            continue

        def t_bc(e, w, inc, bump, tc=tc):
            w("ve", p.cnt["ve"])
            e.matmul(psR[:, :], lhsT=sonesr[:, :], rhs=rowbuf[0:1, 512:1024], start=True, stop=True)
            e.matmul(psM[:, :], lhsT=sonesr[:, :], rhs=rowbuf[0:1, 0:512], start=True, stop=True)
            bump("pe")
        p.emit("tensor", t_bc)
        if ln_steps < 5:
            continue

        def v_apply(e, w, inc, bump, tc=tc):
            w("pe", p.cnt["pe"])
            for ft in range(TF):
                xs = x_fm[:, ft * TOK + tc * 512: ft * TOK + tc * 512 + 512]
                hs = h_fm[:, ft * TOK + tc * 512: ft * TOK + tc * 512 + 512]
                e.tensor_tensor(out=xsq, in0=xs, in1=psR[:, :], op=Alu.mult)
                e.tensor_tensor(out=xsq, in0=xsq[:, :], in1=psM[:, :], op=Alu.add)
                e.tensor_scalar(out=hs, in0=xsq, scalar1=c1_cols[:, ft:ft + 1],
                                scalar2=s_cols[:, ft:ft + 1], op0=Alu.mult, op1=Alu.add)
            bump("ve")
        p.emit("vector", v_apply)


def attn_head(p, sq, sk_ap, sv_ap, smask, pT, psc, pso, pss, pbc, srecip, srbc,
              so_slot, sonesb, sonesr, q_off, st, so_guard):
    exp_done = st["exp"]
    pv_done = st["pv"]
    for qc in range(4):
        nkt = 4 * qc + 4
        for kt in range(nkt):
            slot = kt % 2
            diag = kt >= 4 * qc

            def t_sc(e, w, inc, bump, kt=kt, qc=qc, slot=slot):
                w("ac", exp_done.get((qc, kt - 2), 0))
                if qc == 0 and kt == 0:
                    w("ve", st["pso_free"])
                e.matmul(psc[slot][:, :], lhsT=sk_ap(kt),
                         rhs=sq[:, q_off + qc * 512: q_off + qc * 512 + 512],
                         start=True, stop=True)
                bump("pe")
            p.emit("tensor", t_sc)
            sc_done = p.cnt["pe"]

            if diag:
                def v_mask(e, w, inc, bump, kt=kt, qc=qc, slot=slot, sc_done=sc_done):
                    w("pe", sc_done)
                    e.tensor_tensor(out=psc[slot][:, :], in0=psc[slot][:, :],
                                    in1=smask[:, (kt - 4 * qc) * 512:(kt - 4 * qc) * 512 + 512],
                                    op=Alu.add)
                    bump("ve")
                p.emit("vector", v_mask)
                mask_done = p.cnt["ve"]
            else:
                mask_done = None

            def a_exp(e, w, inc, bump, kt=kt, qc=qc, slot=slot, mask_done=mask_done,
                      sc_done=sc_done):
                w("ve", p.const_ve)
                if mask_done is not None:
                    w("ve", mask_done)
                else:
                    w("pe", sc_done)
                w("pe", pv_done.get((qc, kt - 2), 0))
                e.activation(out=pT[slot][:, :], in_=psc[slot][:, :], func=Act.Exp,
                             bias=ESHIFT, scale=SCALE)
                bump("ac")
            p.emit("scalar", a_exp)
            exp_done[(qc, kt)] = p.cnt["ac"]

            def t_acc(e, w, inc, bump, kt=kt, qc=qc, slot=slot, nkt=nkt):
                w("ac", exp_done[(qc, kt)])
                if kt == 0:
                    w("ve", st["pso_free"])
                e.matmul(pss[:, :], lhsT=sonesb[:, :], rhs=pT[slot][:, :], start=(kt == 0), stop=(kt == nkt - 1))
                e.matmul(pso[:, :], lhsT=sv_ap(kt), rhs=pT[slot][:, :], start=(kt == 0), stop=(kt == nkt - 1))
                bump("pe")
            p.emit("tensor", t_acc)
            pv_done[(qc, kt)] = p.cnt["pe"]

        def v_recip(e, w, inc, bump, qc=qc, nkt=nkt):
            w("pe", pv_done[(qc, nkt - 1)])
            e.reciprocal(out=srecip[:, :], in_=pss[:, :])
            bump("ve")
        p.emit("vector", v_recip)
        recip_ve = p.cnt["ve"]

        def t_rbc(e, w, inc, bump, qc=qc, recip_ve=recip_ve):
            w("ve", recip_ve)
            w("ve", st["pso_free"])
            e.matmul(pbc[:, :], lhsT=sonesr[:, :], rhs=srecip[:, :], start=True, stop=True)
            bump("pe")
        p.emit("tensor", t_rbc)
        rbc_pe = p.cnt["pe"]

        def a_rbc(e, w, inc, bump, qc=qc, rbc_pe=rbc_pe, ve_need=st["pso_free"]):
            w("pe", rbc_pe)
            w("ve", ve_need)
            e.activation(out=srbc[:, :], in_=pbc[:, :], func=Act.Copy)
            bump("ac")
        p.emit("scalar", a_rbc)
        rbc_ac = p.cnt["ac"]

        def v_out(e, w, inc, bump, qc=qc, rbc_ac=rbc_ac):
            w("ac", rbc_ac)
            if qc == 0:
                w("dmo", so_guard)
            e.tensor_tensor(out=so_slot[:, qc * 512: qc * 512 + 512], in0=pso[:, :],
                            in1=srbc[:, :], op=Alu.mult)
            bump("ve")
        p.emit("vector", v_out)
        st["pso_free"] = p.cnt["ve"]


def build_L1():
    p = Prog()
    ccT = p.din("ccT", (D, 8))
    wmod = p.din("wmod", (D, 3072))
    clampsel = p.din("clampsel", (4, 3072))
    gi = p.din("gi", (D, 688), I32)
    ui = p.din("ui", (D, 688), I32)
    di = p.din("di", (I0, 256), I32)
    lutv = p.din("lutv", (1, 16))
    wg32 = p.din("wg32", (2, D, 1024))
    wl32 = p.din("wl32", (1024, D))
    modsh = p.dout("modsh", (4, 3072))
    gw = p.dout("gw", (D, 688), BF16)
    uw = p.dout("uw", (D, 688), BF16)
    dw = p.dout("dw", (I0, 256), BF16)
    wg8 = p.dout("wg8", (2, D, 1024), FP8)
    wl8 = p.dout("wl8", (1024, D), FP8)

    sccT = p.sb("sccT", (128, TF * 8))
    ssil = p.sb("ssil", (128, TF * 8))
    wm = [p.sb(f"wm{i}", (128, 3072)) for i in range(2)]
    smod = p.sb("smod", (4, 3072))
    sclamp = p.sb("sclamp", (4, 3072))
    scl1 = p.sb("scl1", (4, 3072))
    scl2 = p.sb("scl2", (4, 3072))
    slut1 = p.sb("slut1", (1, 16))
    slutB = p.sb("slutB", (128, 16))
    sonesr = p.sb("sonesr", (1, 128))
    sidx = [p.sb(f"sidx{i}", (128, 1024), I32) for i in range(2)]
    sidxb = p.sb("sidxb", (128, 1024), BF16)
    stmp = p.sb("stmp", (128, 1024), BF16)
    sacc = [p.sb(f"sacc{i}", (128, 1024), BF16) for i in range(2)]
    sw32 = [p.sb(f"sw32{i}", (128, 2048)) for i in range(2)]
    sw8 = [p.sb(f"sw8{i}", (128, 2048), FP8) for i in range(2)]
    pm = [p.ps(f"pm{i}", (4, 512)) for i in range(6)]
    plut = p.ps("plut", (128, 16))

    p.start()

    ghost_jobs = []
    for kt in range(TF):
        ghost_jobs.append((gi[kt * 128:(kt + 1) * 128, :], gw[kt * 128:(kt + 1) * 128, :], 688))
    for kt in range(TF):
        ghost_jobs.append((ui[kt * 128:(kt + 1) * 128, :], uw[kt * 128:(kt + 1) * 128, :], 688))
    for kt in range(NT0):
        ghost_jobs.append((di[kt * 128:(kt + 1) * 128, :], dw[kt * 128:(kt + 1) * 128, :], 256))

    def y0(e, w, inc, bump):
        inc(e.dma_start(out=sccT[:].rearrange("p (k m) -> p k m", k=TF, m=8),
                        in_=ccT.rearrange("(k p) m -> p k m", p=128)), "dma")
        inc(e.dma_start(out=slut1[:], in_=lutv), "dma")
        inc(e.dma_start(out=sclamp[:], in_=clampsel), "dma")
        for i in range(2):
            src, _, wd = ghost_jobs[i]
            inc(e.dma_start(out=sidx[i][:, 0:wd], in_=src), "dma")
    p.emit("sync", y0)
    pro_dma = p.cnt["dma"]

    def v0(e, w, inc, bump):
        w("dma", pro_dma)
        e.memset(sonesr[:], 1.0)
        bump("ve")
    p.emit("vector", v0)

    def a0(e, w, inc, bump):
        w("dma", pro_dma)
        e.activation(out=ssil[:], in_=sccT[:], func=Act.Silu)
        bump("ac")
    p.emit("scalar", a0)

    def t0(e, w, inc, bump):
        w("ve", p.cnt["ve"])
        e.matmul(plut[:], lhsT=sonesr[:], rhs=slut1[:], start=True, stop=True)
        bump("pe")
    p.emit("tensor", t0)

    def v1(e, w, inc, bump):
        w("pe", p.cnt["pe"])
        e.tensor_copy(out=slutB[:], in_=plut[:])
        bump("ve")
    p.emit("vector", v1)
    lut_ready = p.cnt["ve"]

    ve_done_job = {}
    dmo_job = {}
    for i, (src, dst, wd) in enumerate(ghost_jobs):
        if i >= 2:
            def y(e, w, inc, bump, i=i, src=src, wd=wd):
                w("ve", ve_done_job.get(i - 2, 0))
                inc(e.dma_start(out=sidx[i % 2][:, 0:wd], in_=src), "dma")
            p.emit("sync", y)
        dma_need = pro_dma + 16 * max(0, i - 1) if i >= 2 else pro_dma
        dma_need = p.cnt["dma"]

        def v(e, w, inc, bump, i=i, wd=wd, dma_need=dma_need):
            w("dma", dma_need)
            w("ve", lut_ready)
            w("dmo", dmo_job.get(i - 2, 0))
            e.tensor_copy(out=sidxb[:, 0:wd], in_=sidx[i % 2][:, 0:wd])
            acc = sacc[i % 2]
            e.tensor_scalar(out=acc[:, 0:wd], in0=sidxb[:, 0:wd], scalar1=0.0,
                            scalar2=slutB[:, 0:1], op0=Alu.is_equal, op1=Alu.mult)
            for j in range(1, 16):
                e.tensor_scalar(out=stmp[:, 0:wd], in0=sidxb[:, 0:wd], scalar1=float(j),
                                scalar2=slutB[:, j:j + 1], op0=Alu.is_equal, op1=Alu.mult)
                e.tensor_tensor(out=acc[:, 0:wd], in0=acc[:, 0:wd], in1=stmp[:, 0:wd], op=Alu.add)
            bump("ve")
        p.emit("vector", v)
        ve_done_job[i] = p.cnt["ve"]

        def g(e, w, inc, bump, i=i, dst=dst, wd=wd):
            w("ve", ve_done_job[i])
            inc(e.dma_start(out=dst, in_=sacc[i % 2][:, 0:wd]), "dmo")
        p.emit("gpsimd", g)
        dmo_job[i] = p.cnt["dmo"]

    pe_kt = {}
    for kt in range(TF):
        def y(e, w, inc, bump, kt=kt):
            w("pe", pe_kt.get(kt - 2, 0))
            inc(e.dma_start(out=wm[kt % 2][:], in_=wmod[kt * 128:(kt + 1) * 128, :]), "dma")
        p.emit("sync", y)
        need = p.cnt["dma"]

        def t(e, w, inc, bump, kt=kt, need=need):
            w("dma", need)
            w("ac", 1)
            for ch in range(6):
                lo = 4 if ch >= 3 else 0
                e.matmul(pm[ch][:], lhsT=ssil[:, kt * 8 + lo: kt * 8 + lo + 4],
                         rhs=wm[kt % 2][:, ch * 512:(ch + 1) * 512],
                         start=(kt == 0), stop=(kt == TF - 1))
            bump("pe")
        p.emit("tensor", t)
        pe_kt[kt] = p.cnt["pe"]

    def v_mod(e, w, inc, bump):
        w("pe", pe_kt[TF - 1])
        for ch in range(6):
            e.tensor_copy(out=smod[:, ch * 512:(ch + 1) * 512], in_=pm[ch][:])
        e.tensor_scalar(out=scl2[:], in0=smod[:], scalar1=0.0, scalar2=2.0,
                        op0=Alu.is_ge, op1=Alu.mult)
        e.tensor_scalar(out=scl2[:], in0=scl2[:], scalar1=1.0, scalar2=None, op0=Alu.subtract)
        e.tensor_tensor(out=scl1[:], in0=smod[:], in1=scl2[:], op=Alu.mult)
        e.tensor_scalar(out=scl1[:], in0=scl1[:], scalar1=0.25, scalar2=None, op0=Alu.max)
        e.tensor_tensor(out=scl1[:], in0=scl1[:], in1=scl2[:], op=Alu.mult)
        e.tensor_tensor(out=scl1[:], in0=scl1[:], in1=smod[:], op=Alu.subtract)
        e.tensor_tensor(out=scl1[:], in0=scl1[:], in1=sclamp[:], op=Alu.mult)
        e.tensor_tensor(out=smod[:], in0=smod[:], in1=scl1[:], op=Alu.add)
        bump("ve")
    p.emit("vector", v_mod)
    mod_ve = p.cnt["ve"]

    def g_mod(e, w, inc, bump):
        w("ve", mod_ve)
        inc(e.dma_start(out=modsh, in_=smod[:]), "dmo")
    p.emit("gpsimd", g_mod)

    cast_jobs = []
    for gidx in range(2):
        for kt in range(TF):
            cast_jobs.append((wg32[gidx, kt * 128:(kt + 1) * 128, :],
                              wg8[gidx, kt * 128:(kt + 1) * 128, :], 1024))
    for kt in range(8):
        cast_jobs.append((wl32[kt * 128:(kt + 1) * 128, :],
                          wl8[kt * 128:(kt + 1) * 128, :], 2048))
    ac_job = {}
    dmo_cast = {}
    for i, (src, dst, wd) in enumerate(cast_jobs):
        def y(e, w, inc, bump, i=i, src=src, wd=wd):
            w("ac", ac_job.get(i - 2, 0))
            inc(e.dma_start(out=sw32[i % 2][:, 0:wd], in_=src), "dma")
        p.emit("sync", y)
        need = p.cnt["dma"]

        def a(e, w, inc, bump, i=i, wd=wd, need=need):
            w("dma", need)
            w("dmo", dmo_cast.get(i - 2, 0))
            e.activation(out=sw8[i % 2][:, 0:wd], in_=sw32[i % 2][:, 0:wd], func=Act.Copy)
            bump("ac")
        p.emit("scalar", a)
        ac_job[i] = p.cnt["ac"]

        def g(e, w, inc, bump, i=i, dst=dst, wd=wd):
            w("ac", ac_job[i])
            inc(e.dma_start(out=dst, in_=sw8[i % 2][:, 0:wd]), "dmo")
        p.emit("gpsimd", g)
        dmo_cast[i] = p.cnt["dmo"]

    return p.finish()


def build_L2(stage=3):
    p = Prog()
    xfm = p.din("xfm", (D, TOK))
    wq_t = p.din("wq_t", (48 * TF * 128, 128))
    bq_cols = p.din("bq_cols", (128, 48))
    ca_cols = p.din("ca_cols", (128, TF))
    sa_cols = p.din("sa_cols", (128, TF))
    r0c = p.din("r0c", (128, TOK))
    r0s = p.din("r0s", (128, TOK))
    qkvp = p.dout("qkvp", (6144, TOK), BF16)

    x_fm = p.sb("x_fm", (128, TF * TOK))
    h_fm = p.sb("h_fm", (128, TF * TOK), BF16)
    xsq = p.sb("xsq", (128, 512))
    xsqa = p.sb("xsqa", (128, 512))
    rowbuf = p.sb("rowbuf", (1, 1024))
    sca = p.sb("sca", (128, TF))
    ssa = p.sb("ssa", (128, TF))
    sbq = p.sb("sbq", (128, 48))
    sr0c = p.sb("sr0c", (128, TOK))
    sr0s = p.sb("sr0s", (128, TOK))
    rotf = p.sb("rotf", (128, 8 * TOK))
    rotout = p.sb("rotout", (128, 8 * TOK), BF16)
    ostage = [p.sb(f"ostage{i}", (128, TOK), BF16) for i in range(4)]
    w32 = [p.sb(f"w32{i}", (128, 2048)) for i in range(2)]
    wbf = [p.sb(f"wbf{i}", (128, 2048), BF16) for i in range(2)]
    ropescr = p.sb("ropescr", (128, 1024))
    sonesc = p.sb("sonesc", (128, 1))
    sonesr = p.sb("sonesr", (1, 128))
    P = [p.ps(f"P{i}", (128, 512)) for i in range(8)]

    p.start()

    def y0(e, w, inc, bump):
        for ft in range(TF):
            inc(e.dma_start(out=x_fm[:, ft * TOK:(ft + 1) * TOK],
                            in_=xfm[ft * 128:(ft + 1) * 128, :]), "dma")
        inc(e.dma_start(out=sca[:], in_=ca_cols), "dma")
        inc(e.dma_start(out=ssa[:], in_=sa_cols), "dma")
        inc(e.dma_start(out=sbq[:], in_=bq_cols), "dma")
        inc(e.dma_start(out=sr0c[:], in_=r0c), "dma")
        inc(e.dma_start(out=sr0s[:], in_=r0s), "dma")
    p.emit("sync", y0)
    pro = p.cnt["dma"]

    def v0(e, w, inc, bump):
        w("dma", pro)
        e.memset(sonesc[:], 1.0)
        e.memset(sonesr[:], 1.0)
        e.tensor_scalar(out=sca[:], in0=sca[:], scalar1=1.0, scalar2=None, op0=Alu.add)
        bump("ve")
    p.emit("vector", v0)
    v0_done = p.cnt["ve"]

    ln_mod_fm(p, x_fm, h_fm, sca, ssa, P[4][0:1, :], P[5][0:1, :], P[6], P[7],
              xsqa, xsq, rowbuf, sonesc, sonesr,
              {"dma": pro, "ve": v0_done})
    hfm_ready = p.cnt["ve"]
    if stage < 2:
        return p.finish()

    cast_done = {}
    pe_done = {}
    evac_done = {}
    dmo_done = {}
    for mt in range(48):
        def y(e, w, inc, bump, mt=mt):
            w("ac", cast_done.get(mt - 2, 0))
            inc(e.dma_start(out=w32[mt % 2][:].rearrange("p (k m) -> p k m", m=128),
                            in_=wq_t[mt * TF * 128:(mt + 1) * TF * 128, :].rearrange("(k p) m -> p k m", p=128)), "dma")
        p.emit("sync", y)
        need = p.cnt["dma"]

        def a(e, w, inc, bump, mt=mt, need=need):
            w("dma", need)
            w("pe", pe_done.get(mt - 2, 0))
            e.activation(out=wbf[mt % 2][:], in_=w32[mt % 2][:], func=Act.Copy)
            bump("ac")
        p.emit("scalar", a)
        cast_done[mt] = p.cnt["ac"]

        def t(e, w, inc, bump, mt=mt):
            w("ac", cast_done[mt])
            w("ve", max(hfm_ready, evac_done.get(mt - 2, 0)))
            for tc in range(2):
                pq = P[(mt % 2) * 2 + tc]
                for kt in range(TF):
                    e.matmul(pq[:, :], lhsT=wbf[mt % 2][:, kt * 128:(kt + 1) * 128],
                             rhs=h_fm[:, kt * TOK + tc * 512: kt * TOK + tc * 512 + 512],
                             start=(kt == 0), stop=(kt == TF - 1))
            bump("pe")
        p.emit("tensor", t)
        pe_done[mt] = p.cnt["pe"]

        def v(e, w, inc, bump, mt=mt):
            w("pe", pe_done[mt])
            if mt >= 8:
                w("dmo", dmo_done.get(mt - 4, 0))
            for tc in range(2):
                pq = P[(mt % 2) * 2 + tc]
                if mt < 8:
                    e.tensor_scalar(out=rotf[:, mt * TOK + tc * 512: mt * TOK + tc * 512 + 512],
                                    in0=pq, scalar1=sbq[:, mt:mt + 1], scalar2=None, op0=Alu.add)
                else:
                    e.tensor_scalar(out=ostage[mt % 4][:, tc * 512:(tc + 1) * 512],
                                    in0=pq, scalar1=sbq[:, mt:mt + 1], scalar2=None, op0=Alu.add)
            bump("ve")
        p.emit("vector", v)
        evac_done[mt] = p.cnt["ve"]

        if mt >= 8:
            def g(e, w, inc, bump, mt=mt):
                w("ve", evac_done[mt])
                inc(e.dma_start(out=qkvp[mt * 128:(mt + 1) * 128, :], in_=ostage[mt % 4][:]), "dmo")
            p.emit("gpsimd", g)
            dmo_done[mt] = p.cnt["dmo"]

        if mt == 7 and stage >= 3:
            def v_rope(e, w, inc, bump):
                w("ve", evac_done[7])
                for (a, b) in ((0, 2), (1, 3), (4, 6), (5, 7)):
                    for tc in range(2):
                        s0 = ropescr[:, 0:512]
                        s1 = ropescr[:, 512:1024]
                        aa = rotf[:, a * TOK + tc * 512: a * TOK + tc * 512 + 512]
                        bb = rotf[:, b * TOK + tc * 512: b * TOK + tc * 512 + 512]
                        cc = sr0c[:, tc * 512:(tc + 1) * 512]
                        ss = sr0s[:, tc * 512:(tc + 1) * 512]
                        e.tensor_tensor(out=s0, in0=aa, in1=cc, op=Alu.mult)
                        e.tensor_tensor(out=s1, in0=bb, in1=ss, op=Alu.mult)
                        e.tensor_tensor(out=rotout[:, a * TOK + tc * 512: a * TOK + tc * 512 + 512],
                                        in0=s0, in1=s1, op=Alu.subtract)
                        e.tensor_tensor(out=s0, in0=bb, in1=cc, op=Alu.mult)
                        e.tensor_tensor(out=s1, in0=aa, in1=ss, op=Alu.mult)
                        e.tensor_tensor(out=rotout[:, b * TOK + tc * 512: b * TOK + tc * 512 + 512],
                                        in0=s0, in1=s1, op=Alu.add)
                bump("ve")
            p.emit("vector", v_rope)
            rope_ve = p.cnt["ve"]

            def g_rope(e, w, inc, bump):
                w("ve", rope_ve)
                inc(e.dma_start(out=qkvp[0:1024, :].rearrange("(n p) t -> p n t", p=128),
                                in_=rotout[:].rearrange("p (n t) -> p n t", n=8)), "dmo")
            p.emit("gpsimd", g_rope)

    return p.finish()


def build_L3():
    p = Prog()
    qf = p.din("qf", (1024, S), BF16)
    kf = p.din("kf", (1024, S), BF16)
    vt = p.din("vt", (8 * S, 128), BF16)
    mask4 = p.din("mask4", (4 * 128, 512))
    of = p.dout("of", (1024, S), BF16)

    sq = p.sb("sq", (128, 8 * S), BF16)
    sk = p.sb("sk", (128, 8 * S), BF16)
    sv = p.sb("sv", (128, 8 * 16 * 128), BF16)
    smask = p.sb("smask", (128, 4 * 512))
    so = [p.sb(f"so{i}", (128, S), BF16) for i in range(2)]
    pT = [p.sb(f"pT{i}", (128, 512), BF16) for i in range(2)]
    srecip = p.sb("srecip", (1, 512))
    srbc = p.sb("srbc", (128, 512))
    sonesb = p.sb("sonesb", (128, 1), BF16)
    sonesr = p.sb("sonesr", (1, 128))
    psc = [p.ps(f"psc{i}", (128, 512)) for i in range(2)]
    pso = p.ps("pso", (128, 512))
    pssf = p.ps("pssf", (1, 512))
    pbc = p.ps("pbc", (128, 512))

    p.start()

    def y0(e, w, inc, bump):
        inc(e.dma_start(out=sq[:].rearrange("p (h t) -> p h t", h=8),
                        in_=qf.rearrange("(h p) t -> p h t", p=128)), "dma")
        inc(e.dma_start(out=sk[:].rearrange("p (h t) -> p h t", h=8),
                        in_=kf.rearrange("(h p) t -> p h t", p=128)), "dma")
        inc(e.dma_start(out=sv[:].rearrange("p (h n d) -> p h n d", h=8, n=16),
                        in_=vt.rearrange("(h n p) d -> p h n d", h=8, p=128)), "dma")
        inc(e.dma_start(out=smask[:].rearrange("p (j q) -> p j q", j=4),
                        in_=mask4.rearrange("(j p) q -> p j q", p=128)), "dma")
    p.emit("sync", y0)
    pro = p.cnt["dma"]

    def v0(e, w, inc, bump):
        w("dma", pro)
        e.memset(sonesb[:], 1.0)
        e.memset(sonesr[:], 1.0)
        bump("ve")
    p.emit("vector", v0)
    v0_done = p.cnt["ve"]

    def t0(e, w, inc, bump):
        w("dma", pro)
        w("ve", v0_done)
    p.emit("tensor", t0)

    dmo_h = {}
    for j in range(8):
        st = {"exp": {}, "pv": {}, "pso_free": p.cnt["ve"]}
        attn_head(p, sq,
                  lambda kt, j=j: sk[:, j * S + kt * 128: j * S + kt * 128 + 128],
                  lambda kt, j=j: sv[:, (j * 16 + kt) * 128: (j * 16 + kt) * 128 + 128],
                  smask, pT, psc, pso, pssf, pbc, srecip, srbc, so[j % 2],
                  sonesb, sonesr, q_off=j * S, st=st, so_guard=dmo_h.get(j - 2, 0))

        def g(e, w, inc, bump, j=j, ve_need=p.cnt["ve"]):
            w("ve", ve_need)
            inc(e.dma_start(out=of[j * 128:(j + 1) * 128, :], in_=so[j % 2][:]), "dmo")
        p.emit("gpsimd", g)
        dmo_h[j] = p.cnt["dmo"]

    return p.finish()


def build_L4():
    p = Prog()
    xfm = p.din("xfm", (D, TOK))
    ofm = p.din("ofm", (D, TOK), BF16)
    wd_t = p.din("wd_t", (TF * TF * 128, 128))
    bd_cols = p.din("bd_cols", (128, TF))
    ga_cols = p.din("ga_cols", (128, TF))
    cb_cols = p.din("cb_cols", (128, TF))
    sbv_cols = p.din("sbv_cols", (128, TF))
    gb_cols = p.din("gb_cols", (128, TF))
    ca1_cols = p.din("ca1_cols", (128, TF))
    sa1_cols = p.din("sa1_cols", (128, TF))
    gwt = p.din("gwt", (NT0 * TF * 128, 128), BF16)
    uwt = p.din("uwt", (NT0 * TF * 128, 128), BF16)
    dwt = p.din("dwt", (TF * NT0 * 128, 128), BF16)
    wl1_t = p.din("wl1_t", (19 * TF * 128, 128))
    r1c = p.din("r1c", (128, TOK))
    r1s = p.din("r1s", (128, TOK))
    x2fm = p.dout("x2fm", (D, TOK))
    q1p = p.dout("q1p", (D, TOK), BF16)
    kp_r = p.dout("kp_r", (128, TOK))
    vp_f = p.dout("vp_f", (128, TOK))
    kvhg = p.dout("kvhg", (128, TOK))

    x_fm = p.sb("x_fm", (128, TF * TOK))
    h_fm = p.sb("h_fm", (128, TF * TOK), BF16)
    hmid = p.sb("hmid", (128, NT0 * 512), BF16)
    xsq = p.sb("xsq", (128, 512))
    xsqa = p.sb("xsqa", (128, 512))
    rowbuf = p.sb("rowbuf", (1, 1024))
    w32 = [p.sb(f"w32{i}", (128, 2048)) for i in range(2)]
    wbf = [p.sb(f"wbf{i}", (128, 2048), BF16) for i in range(2)]
    gbuf = [p.sb(f"gbuf{i}", (128, 2048), BF16) for i in range(2)]
    ubuf = [p.sb(f"ubuf{i}", (128, 2048), BF16) for i in range(2)]
    dwb = p.sb("dwb", (128, NT0 * 128), BF16)
    cols = p.sb("cols", (128, 7 * TF))
    sr1c = p.sb("sr1c", (128, TOK))
    sr1s = p.sb("sr1s", (128, TOK))
    g0 = p.sb("gact0", (128, 512), BF16)
    gact = [g0, g0]
    qstage = [dwb[:, 0:512], dwb[:, 512:1024]]
    fstage = [x_fm[:, 0:512], x_fm[:, 512:1024]]
    sonesc = p.sb("sonesc", (128, 1))
    sonesr = p.sb("sonesr", (1, 128))
    P = [p.ps(f"P{i}", (128, 512)) for i in range(8)]

    p.start()

    def col(i):
        return cols[:, i * TF:(i + 1) * TF]

    def y0(e, w, inc, bump):
        for ft in range(TF):
            inc(e.dma_start(out=x_fm[:, ft * TOK:(ft + 1) * TOK],
                            in_=xfm[ft * 128:(ft + 1) * 128, :]), "dma")
        for ft in range(TF):
            inc(e.dma_start(out=h_fm[:, ft * TOK:(ft + 1) * TOK],
                            in_=ofm[ft * 128:(ft + 1) * 128, :]), "dma")
        for i, t in enumerate((bd_cols, ga_cols, cb_cols, sbv_cols, gb_cols, ca1_cols, sa1_cols)):
            inc(e.dma_start(out=cols[:, i * TF:(i + 1) * TF], in_=t), "dma")
        inc(e.dma_start(out=sr1c[:], in_=r1c), "dma")
        inc(e.dma_start(out=sr1s[:], in_=r1s), "dma")
    p.emit("sync", y0)
    pro = p.cnt["dma"]

    def v0(e, w, inc, bump):
        w("dma", pro)
        e.memset(sonesc[:], 1.0)
        e.memset(sonesr[:], 1.0)
        e.tensor_scalar(out=col(2), in0=col(2), scalar1=1.0, scalar2=None, op0=Alu.add)
        e.tensor_scalar(out=col(5), in0=col(5), scalar1=1.0, scalar2=None, op0=Alu.add)
        bump("ve")
    p.emit("vector", v0)

    def proj_stream(wsrc, nmt, consume):
        cast_done = {}
        pe_done = {}
        for mt in range(nmt):
            def y(e, w, inc, bump, mt=mt):
                w("ac", cast_done.get(mt - 2, 0))
                inc(e.dma_start(out=w32[mt % 2][:].rearrange("p (k m) -> p k m", m=128),
                                in_=wsrc[mt * TF * 128:(mt + 1) * TF * 128, :].rearrange("(k p) m -> p k m", p=128)), "dma")
            p.emit("sync", y)
            need = p.cnt["dma"]

            def a(e, w, inc, bump, mt=mt, need=need):
                w("dma", need)
                w("pe", pe_done.get(mt - 2, 0))
                e.activation(out=wbf[mt % 2][:], in_=w32[mt % 2][:], func=Act.Copy)
                bump("ac")
            p.emit("scalar", a)
            cast_done[mt] = p.cnt["ac"]

            def t(e, w, inc, bump, mt=mt):
                w("ac", cast_done[mt])
                w("ve", p.cnt["ve"])
                for tc in range(2):
                    pq = P[(mt % 2) * 2 + tc]
                    for kt in range(TF):
                        e.matmul(pq[:, :], lhsT=wbf[mt % 2][:, kt * 128:(kt + 1) * 128],
                                 rhs=h_fm[:, kt * TOK + tc * 512: kt * TOK + tc * 512 + 512],
                                 start=(kt == 0), stop=(kt == TF - 1))
                bump("pe")
            p.emit("tensor", t)
            pe_done[mt] = p.cnt["pe"]
            consume(mt, pe_done[mt])

    def dense_consume(mt, pe_need):
        def v(e, w, inc, bump, mt=mt, pe_need=pe_need):
            w("pe", pe_need)
            for tc in range(2):
                pq = P[(mt % 2) * 2 + tc]
                xsl = x_fm[:, mt * TOK + tc * 512: mt * TOK + tc * 512 + 512]
                e.tensor_scalar(out=xsq[:, :], in0=pq[:, :], scalar1=col(0)[:, mt:mt + 1],
                                scalar2=col(1)[:, mt:mt + 1], op0=Alu.add, op1=Alu.mult)
                e.tensor_tensor(out=xsl, in0=xsl, in1=xsq[:, :], op=Alu.add)
            bump("ve")
        p.emit("vector", v)

    proj_stream(wd_t, TF, dense_consume)

    ln_mod_fm(p, x_fm, h_fm, col(2), col(3), P[4][0:1, :], P[5][0:1, :], P[6], P[7],
              xsqa, xsq, rowbuf, sonesc, sonesr, {"ve": p.cnt["ve"]})
    h2_ready = p.cnt["ve"]

    for tc in range(2):
        ve0, ac0, pe0 = p.cnt["ve"], p.cnt["ac"], p.cnt["pe"]
        silu_done = {}
        mult_done = {}
        pe_gu = {}
        for mt in range(NT0):
            def y(e, w, inc, bump, mt=mt, tc=tc, pe0=pe0):
                w("pe", pe_gu.get(mt - 2, pe0))
                inc(e.dma_start(out=gbuf[mt % 2][:].rearrange("p (k m) -> p k m", m=128),
                                in_=gwt[mt * TF * 128:(mt + 1) * TF * 128, :].rearrange("(k p) m -> p k m", p=128)), "dma")
                inc(e.dma_start(out=ubuf[mt % 2][:].rearrange("p (k m) -> p k m", m=128),
                                in_=uwt[mt * TF * 128:(mt + 1) * TF * 128, :].rearrange("(k p) m -> p k m", p=128)), "dma")
            p.emit("sync", y)
            need = p.cnt["dma"]

            def t(e, w, inc, bump, mt=mt, tc=tc, need=need, ve0=ve0, ac0=ac0):
                w("dma", need)
                w("ve", max(h2_ready, mult_done.get(mt - 2, ve0)))
                w("ac", silu_done.get(mt - 2, ac0))
                pg = P[(mt % 2)]
                pu = P[2 + (mt % 2)]
                for kt in range(TF):
                    e.matmul(pg[:, :], lhsT=gbuf[mt % 2][:, kt * 128:(kt + 1) * 128],
                             rhs=h_fm[:, kt * TOK + tc * 512: kt * TOK + tc * 512 + 512],
                             start=(kt == 0), stop=(kt == TF - 1))
                for kt in range(TF):
                    e.matmul(pu[:, :], lhsT=ubuf[mt % 2][:, kt * 128:(kt + 1) * 128],
                             rhs=h_fm[:, kt * TOK + tc * 512: kt * TOK + tc * 512 + 512],
                             start=(kt == 0), stop=(kt == TF - 1))
                bump("pe")
            p.emit("tensor", t)
            pe_gu[mt] = p.cnt["pe"]

            def a(e, w, inc, bump, mt=mt, ve0=ve0):
                w("pe", pe_gu[mt])
                w("ve", mult_done.get(mt - 1, ve0))
                e.activation(out=gact[mt % 2][:], in_=P[mt % 2][:, :], func=Act.Silu)
                bump("ac")
            p.emit("scalar", a)
            silu_done[mt] = p.cnt["ac"]

            def v(e, w, inc, bump, mt=mt):
                w("ac", silu_done[mt])
                e.tensor_tensor(out=hmid[:, mt * 512:(mt + 1) * 512], in0=P[2 + (mt % 2)][:, :],
                                in1=gact[mt % 2][:], op=Alu.mult)
                bump("ve")
            p.emit("vector", v)
            mult_done[mt] = p.cnt["ve"]

        hmid_ready = p.cnt["ve"]
        ped0, ved0 = p.cnt["pe"], p.cnt["ve"]
        down_evac = {}
        down_pe = {}
        for mt in range(TF):
            def y(e, w, inc, bump, mt=mt, tc=tc, ped0=ped0):
                w("pe", down_pe.get(mt - 1, ped0))
                inc(e.dma_start(out=dwb[:].rearrange("p (k m) -> p k m", m=128),
                                in_=dwt[mt * NT0 * 128:(mt + 1) * NT0 * 128, :].rearrange("(k p) m -> p k m", p=128)), "dma")
            p.emit("sync", y)
            need = p.cnt["dma"]

            def t(e, w, inc, bump, mt=mt, tc=tc, need=need, ved0=ved0):
                w("dma", need)
                w("ve", max(hmid_ready, down_evac.get(mt - 2, ved0)))
                pd = P[4 + (mt % 2)]
                for kt in range(NT0):
                    e.matmul(pd[:, :], lhsT=dwb[:, kt * 128:(kt + 1) * 128],
                             rhs=hmid[:, kt * 512:(kt + 1) * 512],
                             start=(kt == 0), stop=(kt == NT0 - 1))
                bump("pe")
            p.emit("tensor", t)
            down_pe[mt] = p.cnt["pe"]

            def v(e, w, inc, bump, mt=mt, tc=tc):
                w("pe", down_pe[mt])
                pd = P[4 + (mt % 2)]
                xsl = x_fm[:, mt * TOK + tc * 512: mt * TOK + tc * 512 + 512]
                e.tensor_scalar(out=xsq[:, :], in0=pd[:, :], scalar1=col(4)[:, mt:mt + 1],
                                scalar2=None, op0=Alu.mult)
                e.tensor_tensor(out=xsl, in0=xsl, in1=xsq[:, :], op=Alu.add)
                bump("ve")
            p.emit("vector", v)
            down_evac[mt] = p.cnt["ve"]

    x2_ready = p.cnt["ve"]

    def g_x2(e, w, inc, bump):
        w("ve", x2_ready)
        for ft in range(TF):
            inc(e.dma_start(out=x2fm[ft * 128:(ft + 1) * 128, :],
                            in_=x_fm[:, ft * TOK:(ft + 1) * TOK]), "dmo")
    p.emit("gpsimd", g_x2)
    gx2_dmo = p.cnt["dmo"]

    ln_mod_fm(p, x_fm, h_fm, col(5), col(6), P[4][0:1, :], P[5][0:1, :], P[6], P[7],
              xsqa, xsq, rowbuf, sonesc, sonesr, {"ve": p.cnt["ve"]})

    stage_free = {}

    def l1_consume(mt, pe_need):
        def v(e, w, inc, bump, mt=mt, pe_need=pe_need):
            w("pe", pe_need)
            w("dmo", stage_free.get(mt - 2, gx2_dmo))
            for tc in range(2):
                pq = P[(mt % 2) * 2 + tc]
                if mt < 16 or mt == 16:
                    s0 = xsqa[:, :]
                    s1 = xsq[:, :]
                    cc0 = sr1c[0:64, tc * 512:(tc + 1) * 512]
                    ss0 = sr1s[0:64, tc * 512:(tc + 1) * 512]
                    cc1 = sr1c[64:128, tc * 512:(tc + 1) * 512]
                    ss1 = sr1s[64:128, tc * 512:(tc + 1) * 512]
                    dst = (qstage[tc][:] if mt < 16 else fstage[tc][:])
                    e.tensor_tensor(out=s0[0:64, :], in0=pq[0:64, :], in1=cc0, op=Alu.mult)
                    e.tensor_tensor(out=s1[0:64, :], in0=pq[64:128, :], in1=ss0, op=Alu.mult)
                    e.tensor_tensor(out=dst[0:64, :], in0=s0[0:64, :], in1=s1[0:64, :],
                                    op=Alu.subtract)
                    e.tensor_tensor(out=s0[64:128, :], in0=pq[64:128, :], in1=cc1, op=Alu.mult)
                    e.tensor_tensor(out=s1[64:128, :], in0=pq[0:64, :], in1=ss1, op=Alu.mult)
                    e.tensor_tensor(out=dst[64:128, :], in0=s0[64:128, :], in1=s1[64:128, :],
                                    op=Alu.add)
                else:
                    e.tensor_copy(out=fstage[tc][:], in_=pq[:, :])
            bump("ve")
        p.emit("vector", v)
        ve_need = p.cnt["ve"]

        def g(e, w, inc, bump, mt=mt, ve_need=ve_need):
            w("ve", ve_need)
            for tc in range(2):
                if mt < 16:
                    inc(e.dma_start(out=q1p[mt * 128:(mt + 1) * 128, tc * 512:(tc + 1) * 512],
                                    in_=qstage[tc][:]), "dmo")
                elif mt == 16:
                    inc(e.dma_start(out=kp_r[:, tc * 512:(tc + 1) * 512],
                                    in_=fstage[tc][:]), "dmo")
                elif mt == 17:
                    inc(e.dma_start(out=vp_f[:, tc * 512:(tc + 1) * 512],
                                    in_=fstage[tc][:]), "dmo")
                else:
                    inc(e.dma_start(out=kvhg[:, tc * 512:(tc + 1) * 512],
                                    in_=fstage[tc][:]), "dmo")
        p.emit("gpsimd", g)
        stage_free[mt] = p.cnt["dmo"]

    proj_stream(wl1_t, 19, l1_consume)

    return p.finish()


def build_L5():
    p = Prog()
    q1 = p.din("q1", (1024, S), BF16)
    kpr = p.din("kpr", (128, S))
    vpt = p.din("vpt", (S, 128))
    kvhg = p.din("kvhg", (128, S))
    mask4 = p.din("mask4", (4 * 128, 512))
    ehrow = p.din("ehrow", (16, 8 * 128))
    identf = p.din("identf", (128, 128))
    o1 = p.dout("o1", (1024, S), BF16)

    sq = p.sb("sq", (128, 8 * S), BF16)
    skp = p.sb("skp", (128, S))
    svp = p.sb("svp", (128, 16 * 128))
    skvhg = p.sb("skvhg", (128, S))
    scanA = p.sb("scanA", (128, S))
    scanB = p.sb("scanB", (128, S))
    sones2 = p.sb("sones2", (128, S))
    smask = p.sb("smask", (128, 4 * 512))
    sehrow = p.sb("sehrow", (16, 8 * 128))
    sidf = p.sb("sidf", (128, 128))
    svhT = p.sb("svhT", (128, 256))
    sk_h = [p.sb(f"sk_h{i}", (128, S), BF16) for i in range(2)]
    sv_h = [p.sb(f"sv_h{i}", (128, 16 * 128), BF16) for i in range(2)]
    so = [p.sb(f"so{i}", (128, S), BF16) for i in range(2)]
    pT = [p.sb(f"pT{i}", (128, 512), BF16) for i in range(2)]
    srecip = p.sb("srecip", (1, 512))
    srbc = p.sb("srbc", (128, 512))
    sonesb = p.sb("sonesb", (128, 1), BF16)
    sonesr = p.sb("sonesr", (1, 128))
    psc = [p.ps(f"psc{i}", (128, 512)) for i in range(2)]
    pso = p.ps("pso", (128, 512))
    pssf = p.ps("pssf", (1, 512))
    pbc = p.ps("pbc", (128, 512))
    pvh = p.ps("pvh", (128, 16))

    p.start()

    def y0(e, w, inc, bump):
        inc(e.dma_start(out=sq[:].rearrange("p (h t) -> p h t", h=8),
                        in_=q1.rearrange("(h p) t -> p h t", p=128)), "dma")
        inc(e.dma_start(out=skp[:], in_=kpr), "dma")
        inc(e.dma_start(out=svp[:].rearrange("p (n d) -> p n d", n=16),
                        in_=vpt.rearrange("(n p) d -> p n d", p=128)), "dma")
        inc(e.dma_start(out=skvhg[:], in_=kvhg), "dma")
        inc(e.dma_start(out=smask[:].rearrange("p (j q) -> p j q", j=4),
                        in_=mask4.rearrange("(j p) q -> p j q", p=128)), "dma")
        inc(e.dma_start(out=sehrow[:], in_=ehrow), "dma")
        inc(e.dma_start(out=sidf[:], in_=identf), "dma")
    p.emit("sync", y0)
    pro = p.cnt["dma"]

    def a_sig(e, w, inc, bump):
        w("dma", pro)
        e.activation(out=scanB[0:16, :], in_=skvhg[64:80, :], func=Act.Sigmoid)
        e.activation(out=scanA[0:16, :], in_=scanB[0:16, :], func=Act.Ln)
        e.activation(out=scanB[32:48, :], in_=skvhg[96:112, :], func=Act.Sigmoid)
        e.activation(out=scanA[32:48, :], in_=scanB[32:48, :], func=Act.Ln)
        bump("ac")
    p.emit("scalar", a_sig)
    sig_done = p.cnt["ac"]

    def v_scan1(e, w, inc, bump):
        w("dma", pro)
        w("ac", sig_done)
        e.memset(sones2[:], 1.0)
        e.memset(sonesb[:], 1.0)
        e.memset(sonesr[:], 1.0)
        e.memset(scanB[0:16, 0:1], 0.0)
        e.memset(scanB[32:48, 0:1], 0.0)
        e.tensor_tensor_scan(out=scanB[0:16, 1:S], data0=sones2[0:16, 0:S - 1],
                             data1=scanA[0:16, 0:S - 1], initial=0.0,
                             op0=Alu.mult, op1=Alu.add)
        e.tensor_tensor_scan(out=scanB[32:48, 1:S], data0=sones2[32:48, 0:S - 1],
                             data1=scanA[32:48, 0:S - 1], initial=0.0,
                             op0=Alu.mult, op1=Alu.add)
        bump("ve")
    p.emit("vector", v_scan1)
    scan1_done = p.cnt["ve"]

    def a_exp2(e, w, inc, bump):
        w("ve", scan1_done)
        e.activation(out=scanA[0:16, :], in_=scanB[0:16, :], func=Act.Exp)
        e.activation(out=scanA[32:48, :], in_=scanB[32:48, :], func=Act.Exp)
        bump("ac")
    p.emit("scalar", a_exp2)
    exp2_done = p.cnt["ac"]

    def v_scan2(e, w, inc, bump):
        w("ac", exp2_done)
        e.tensor_tensor(out=scanA[64:80, :], in0=skvhg[0:16, :],
                        in1=scanA[0:16, :], op=Alu.mult)
        e.tensor_tensor(out=scanA[96:112, :], in0=skvhg[32:48, :],
                        in1=scanA[32:48, :], op=Alu.mult)
        e.tensor_tensor_scan(out=scanB[0:16, :], data0=sones2[64:80, :],
                             data1=scanA[64:80, :], initial=0.0,
                             op0=Alu.mult, op1=Alu.add)
        e.tensor_tensor_scan(out=scanB[32:48, :], data0=sones2[96:112, :],
                             data1=scanA[96:112, :], initial=0.0,
                             op0=Alu.mult, op1=Alu.add)
        bump("ve")
    p.emit("vector", v_scan2)
    scan_done = p.cnt["ve"]

    vh_ve = {}
    for kt in range(16):
        def t2(e, w, inc, bump, kt=kt):
            w("ve", scan_done if kt == 0 else vh_ve[kt - 1])
            e.matmul(pvh[:, 0:16], lhsT=scanB[32:48, kt * 128:(kt + 1) * 128],
                     rhs=sidf[32:48, 32:48], start=True, stop=True)
            bump("pe")
        p.emit("tensor", t2)
        t2_pe = p.cnt["pe"]

        def v2(e, w, inc, bump, kt=kt, t2_pe=t2_pe):
            w("pe", t2_pe)
            e.tensor_copy(out=svhT[:, kt * 16:(kt + 1) * 16], in_=pvh[:, 0:16])
            bump("ve")
        p.emit("vector", v2)
        vh_ve[kt] = p.cnt["ve"]

    dmo_h = {}
    for j in range(8):
        for ch in range(4):
            def t1(e, w, inc, bump, j=j, ch=ch, ve_need=p.cnt["ve"]):
                w("ve", ve_need)
                e.matmul(pbc[:, :], lhsT=sehrow[:, j * 128:(j + 1) * 128],
                         rhs=scanB[0:16, ch * 512:(ch + 1) * 512], start=True, stop=True)
                bump("pe")
            p.emit("tensor", t1)
            t1_pe = p.cnt["pe"]

            def v1(e, w, inc, bump, j=j, ch=ch, t1_pe=t1_pe):
                w("pe", t1_pe)
                e.tensor_tensor(out=sk_h[j % 2][:, ch * 512:(ch + 1) * 512],
                                in0=pbc[:, :], in1=skp[:, ch * 512:(ch + 1) * 512], op=Alu.mult)
                bump("ve")
            p.emit("vector", v1)

        def v3(e, w, inc, bump, j=j, ve_need=p.cnt["ve"]):
            w("ve", ve_need)
            w("dmo", dmo_h.get(j - 2, 0))
            for kt in range(16):
                e.tensor_scalar(out=sv_h[j % 2][:, kt * 128:(kt + 1) * 128],
                                in0=svp[:, kt * 128:(kt + 1) * 128],
                                scalar1=svhT[:, kt * 16 + j: kt * 16 + j + 1],
                                scalar2=None, op0=Alu.mult)
            bump("ve")
        p.emit("vector", v3)

        st = {"exp": {}, "pv": {}, "pso_free": p.cnt["ve"]}
        attn_head(p, sq,
                  lambda kt, j=j: sk_h[j % 2][:, kt * 128:(kt + 1) * 128],
                  lambda kt, j=j: sv_h[j % 2][:, kt * 128:(kt + 1) * 128],
                  smask, pT, psc, pso, pssf, pbc, srecip, srbc, so[j % 2],
                  sonesb, sonesr, q_off=j * S, st=st, so_guard=dmo_h.get(j - 2, 0))

        def g(e, w, inc, bump, j=j, ve_need=p.cnt["ve"]):
            w("ve", ve_need)
            inc(e.dma_start(out=o1[j * 128:(j + 1) * 128, :], in_=so[j % 2][:]), "dmo")
        p.emit("gpsimd", g)
        dmo_h[j] = p.cnt["dmo"]

    return p.finish()


def build_L6():
    p = Prog()
    xfm = p.din("xfm", (D, TOK))
    ofm = p.din("ofm", (D, TOK), BF16)
    wo_t = p.din("wo_t", (TF * TF * 128, 128))
    ga_cols = p.din("ga_cols", (128, TF))
    cb_cols = p.din("cb_cols", (128, TF))
    sbv_cols = p.din("sbv_cols", (128, TF))
    gb_cols = p.din("gb_cols", (128, TF))
    w8g_t = p.din("w8g_t", (2 * NT1 * TF * 128, 128), FP8)
    wl8_t = p.din("wl8_t", (TF * NT1 * 128, 128), FP8)
    identf = p.din("identf", (128, 128))
    xout = p.dout("xout", (TOK, D))

    x_fm = p.sb("x_fm", (128, TF * TOK))
    o_fm = p.sb("o_fm", (128, TF * TOK), BF16)
    h8 = p.sb("h8", (128, TF * TOK), FP8)
    hmid8 = p.sb("hmid8", (128, NT1 * 512), FP8)
    xsq = p.sb("xsq", (128, 512))
    xsqa = p.sb("xsqa", (128, 512))
    rowbuf = p.sb("rowbuf", (1, 1024))
    w32 = [p.sb(f"w32{i}", (128, 2048)) for i in range(2)]
    wbf = [p.sb(f"wbf{i}", (128, 2048), BF16) for i in range(2)]
    s8g = [p.sb(f"s8g{i}", (128, 2048), FP8) for i in range(2)]
    s8u = [p.sb(f"s8u{i}", (128, 2048), FP8) for i in range(2)]
    swl = [p.sb(f"swl{i}", (128, NT1 * 128), FP8) for i in range(2)]
    cols = p.sb("cols", (128, 4 * TF))
    gact = [p.sb(f"gact{i}", (128, 512), BF16) for i in range(2)]
    sidf = p.sb("sidf", (128, 128))
    xstage = w32
    sonesc = p.sb("sonesc", (128, 1))
    sonesr = p.sb("sonesr", (1, 128))
    P = [p.ps(f"P{i}", (128, 512)) for i in range(8)]

    p.start()

    def col(i):
        return cols[:, i * TF:(i + 1) * TF]

    def y0(e, w, inc, bump):
        for ft in range(TF):
            inc(e.dma_start(out=x_fm[:, ft * TOK:(ft + 1) * TOK],
                            in_=xfm[ft * 128:(ft + 1) * 128, :]), "dma")
        for ft in range(TF):
            inc(e.dma_start(out=o_fm[:, ft * TOK:(ft + 1) * TOK],
                            in_=ofm[ft * 128:(ft + 1) * 128, :]), "dma")
        for i, t in enumerate((ga_cols, cb_cols, sbv_cols, gb_cols)):
            inc(e.dma_start(out=cols[:, i * TF:(i + 1) * TF], in_=t), "dma")
        inc(e.dma_start(out=sidf[:], in_=identf), "dma")
    p.emit("sync", y0)
    pro = p.cnt["dma"]

    def v0(e, w, inc, bump):
        w("dma", pro)
        e.memset(sonesc[:], 1.0)
        e.memset(sonesr[:], 1.0)
        e.tensor_scalar(out=col(1), in0=col(1), scalar1=1.0, scalar2=None, op0=Alu.add)
        bump("ve")
    p.emit("vector", v0)

    cast_done = {}
    pe_done = {}
    for mt in range(TF):
        def y(e, w, inc, bump, mt=mt):
            w("ac", cast_done.get(mt - 2, 0))
            inc(e.dma_start(out=w32[mt % 2][:].rearrange("p (k m) -> p k m", m=128),
                            in_=wo_t[mt * TF * 128:(mt + 1) * TF * 128, :].rearrange("(k p) m -> p k m", p=128)), "dma")
        p.emit("sync", y)
        need = p.cnt["dma"]

        def a(e, w, inc, bump, mt=mt, need=need):
            w("dma", need)
            w("pe", pe_done.get(mt - 2, 0))
            e.activation(out=wbf[mt % 2][:], in_=w32[mt % 2][:], func=Act.Copy)
            bump("ac")
        p.emit("scalar", a)
        cast_done[mt] = p.cnt["ac"]

        def t(e, w, inc, bump, mt=mt):
            w("ac", cast_done[mt])
            w("ve", p.cnt["ve"])
            for tc in range(2):
                pq = P[(mt % 2) * 2 + tc]
                for kt in range(TF):
                    e.matmul(pq[:, :], lhsT=wbf[mt % 2][:, kt * 128:(kt + 1) * 128],
                             rhs=o_fm[:, kt * TOK + tc * 512: kt * TOK + tc * 512 + 512],
                             start=(kt == 0), stop=(kt == TF - 1))
            bump("pe")
        p.emit("tensor", t)
        pe_done[mt] = p.cnt["pe"]

        def v(e, w, inc, bump, mt=mt):
            w("pe", pe_done[mt])
            for tc in range(2):
                pq = P[(mt % 2) * 2 + tc]
                xsl = x_fm[:, mt * TOK + tc * 512: mt * TOK + tc * 512 + 512]
                e.tensor_scalar(out=xsq[:, :], in0=pq[:, :], scalar1=col(0)[:, mt:mt + 1],
                                scalar2=None, op0=Alu.mult)
                e.tensor_tensor(out=xsl, in0=xsl, in1=xsq[:, :], op=Alu.add)
            bump("ve")
        p.emit("vector", v)

    ln_mod_fm(p, x_fm, h8, col(1), col(2), P[4][0:1, :], P[5][0:1, :], P[6], P[7],
              xsqa, xsq, rowbuf, sonesc, sonesr, {"ve": p.cnt["ve"]})
    h8_ready = p.cnt["ve"]

    for tc in range(2):
        ve0, ac0, pe0 = p.cnt["ve"], p.cnt["ac"], p.cnt["pe"]
        gelu_done = {}
        mult_done = {}
        pe_gu = {}
        for mt in range(NT1):
            def y(e, w, inc, bump, mt=mt, tc=tc, pe0=pe0):
                w("pe", pe_gu.get(mt - 2, pe0))
                inc(e.dma_start(out=s8g[mt % 2][:].rearrange("p (k m) -> p k m", m=128),
                                in_=w8g_t[mt * TF * 128:(mt + 1) * TF * 128, :].rearrange("(k p) m -> p k m", p=128)), "dma")
                inc(e.dma_start(out=s8u[mt % 2][:].rearrange("p (k m) -> p k m", m=128),
                                in_=w8g_t[(NT1 + mt) * TF * 128:(NT1 + mt + 1) * TF * 128, :].rearrange("(k p) m -> p k m", p=128)), "dma")
            p.emit("sync", y)
            need = p.cnt["dma"]

            def t(e, w, inc, bump, mt=mt, tc=tc, need=need, ve0=ve0, ac0=ac0):
                w("dma", need)
                w("ve", max(h8_ready, mult_done.get(mt - 2, ve0)))
                w("ac", gelu_done.get(mt - 2, ac0))
                pg = P[(mt % 2)]
                pu = P[2 + (mt % 2)]
                for kt in range(TF):
                    e.matmul(pg[:, :], lhsT=s8g[mt % 2][:, kt * 128:(kt + 1) * 128],
                             rhs=h8[:, kt * TOK + tc * 512: kt * TOK + tc * 512 + 512],
                             start=(kt == 0), stop=(kt == TF - 1))
                for kt in range(TF):
                    e.matmul(pu[:, :], lhsT=s8u[mt % 2][:, kt * 128:(kt + 1) * 128],
                             rhs=h8[:, kt * TOK + tc * 512: kt * TOK + tc * 512 + 512],
                             start=(kt == 0), stop=(kt == TF - 1))
                bump("pe")
            p.emit("tensor", t)
            pe_gu[mt] = p.cnt["pe"]

            def a(e, w, inc, bump, mt=mt, ve0=ve0):
                w("pe", pe_gu[mt])
                w("ve", mult_done.get(mt - 1, ve0))
                e.activation(out=gact[mt % 2][:], in_=P[mt % 2][:, :], func=Act.Gelu_apprx_tanh)
                bump("ac")
            p.emit("scalar", a)
            gelu_done[mt] = p.cnt["ac"]

            def v(e, w, inc, bump, mt=mt):
                w("ac", gelu_done[mt])
                e.tensor_tensor(out=hmid8[:, mt * 512:(mt + 1) * 512], in0=P[2 + (mt % 2)][:, :],
                                in1=gact[mt % 2][:], op=Alu.mult)
                bump("ve")
            p.emit("vector", v)
            mult_done[mt] = p.cnt["ve"]

        hmid_ready = p.cnt["ve"]
        ped0, ved0 = p.cnt["pe"], p.cnt["ve"]
        down_evac = {}
        down_pe = {}
        dma_wl = {}
        for mt in range(TF):
            def y(e, w, inc, bump, mt=mt, tc=tc, ped0=ped0):
                w("pe", down_pe.get(mt - 2, ped0))
                inc(e.dma_start(out=swl[mt % 2][:].rearrange("p (k m) -> p k m", m=128),
                                in_=wl8_t[mt * NT1 * 128:(mt + 1) * NT1 * 128, :].rearrange("(k p) m -> p k m", p=128)), "dma")
            p.emit("sync", y)
            need = p.cnt["dma"]

            def t(e, w, inc, bump, mt=mt, tc=tc, need=need, ved0=ved0):
                w("dma", need)
                w("ve", max(hmid_ready, down_evac.get(mt - 2, ved0)))
                pd = P[4 + (mt % 2)]
                for kt in range(NT1):
                    e.matmul(pd[:, :], lhsT=swl[mt % 2][:, kt * 128:(kt + 1) * 128],
                             rhs=hmid8[:, kt * 512:(kt + 1) * 512],
                             start=(kt == 0), stop=(kt == NT1 - 1))
                bump("pe")
            p.emit("tensor", t)
            down_pe[mt] = p.cnt["pe"]

            def v(e, w, inc, bump, mt=mt, tc=tc):
                w("pe", down_pe[mt])
                pd = P[4 + (mt % 2)]
                xsl = x_fm[:, mt * TOK + tc * 512: mt * TOK + tc * 512 + 512]
                e.tensor_scalar(out=xsq[:, :], in0=pd[:, :], scalar1=col(3)[:, mt:mt + 1],
                                scalar2=None, op0=Alu.mult)
                e.tensor_tensor(out=xsl, in0=xsl, in1=xsq[:, :], op=Alu.add)
                bump("ve")
            p.emit("vector", v)
            down_evac[mt] = p.cnt["ve"]

    x3_ready = p.cnt["ve"]

    dmo_tt = {}
    ps_free = {}
    for tt in range(8):
        copies = []
        for g4 in range(4):
            def t(e, w, inc, bump, tt=tt, g4=g4, ve_need=ps_free.get(g4 % 2, 0)):
                w("ve", max(x3_ready, ve_need))
                pq = P[g4 % 2]
                for i in range(4):
                    ft = g4 * 4 + i
                    e.matmul(pq[:, i * 128:(i + 1) * 128],
                             lhsT=x_fm[:, ft * TOK + tt * 128: ft * TOK + tt * 128 + 128],
                             rhs=sidf[:], start=True, stop=True)
                bump("pe")
            p.emit("tensor", t)
            pe_g = p.cnt["pe"]

            def v(e, w, inc, bump, tt=tt, g4=g4, pe_g=pe_g):
                w("pe", pe_g)
                w("dmo", dmo_tt.get(tt - 2, 0))
                e.tensor_copy(out=xstage[tt % 2][:, g4 * 512:(g4 + 1) * 512],
                              in_=P[g4 % 2][:, :])
                bump("ve")
            p.emit("vector", v)
            ps_free[g4 % 2] = p.cnt["ve"]
        stage_ve = p.cnt["ve"]

        def g(e, w, inc, bump, tt=tt, stage_ve=stage_ve):
            w("ve", stage_ve)
            inc(e.dma_start(out=xout[tt * 128:(tt + 1) * 128, :], in_=xstage[tt % 2][:]), "dmo")
        p.emit("gpsimd", g)
        dmo_tt[tt] = p.cnt["dmo"]

    return p.finish()


def tile_lhsT(w, KT, MT):
    K, M = w.shape
    assert K == KT * 128 and M == MT * 128
    return np.ascontiguousarray(
        w.reshape(KT, 128, MT, 128).transpose(2, 0, 1, 3).reshape(MT * KT * 128, 128))


def cols16(vec):
    return np.ascontiguousarray(vec.reshape(TF, 128).T)


def build_qkv_perm():
    qr1 = [h * 384 + d for h in range(16) for d in range(16)]
    qr2 = [h * 384 + 16 + d for h in range(16) for d in range(16)]
    kr1 = [h * 384 + 128 + d for h in range(16) for d in range(16)]
    kr2 = [h * 384 + 144 + d for h in range(16) for d in range(16)]
    qpass = [h * 384 + 32 + d for h in range(16) for d in range(96)]
    kpass = [h * 384 + 160 + d for h in range(16) for d in range(96)]
    vall = [h * 384 + 256 + d for h in range(16) for d in range(128)]
    return np.array(qr1 + qr2 + kr1 + kr2 + qpass + kpass + vall, dtype=np.int64)


def mask4_np():
    m = np.zeros((4, 128, 512), dtype=np.float32)
    kl = np.arange(128)[:, None]
    ql = np.arange(512)[None, :]
    for j in range(4):
        m[j] = np.where(128 * j + kl <= ql, 0.0, NEG)
    return m.reshape(4 * 128, 512)


def kernel(**inputs):
    f32 = np.float32
    x0 = np.asarray(inputs["L0_attn_hidden_states"], dtype=f32)

    cores = [(c // 2, c % 2) for c in range(8)]

    ccT = np.ascontiguousarray(
        np.concatenate([np.asarray(inputs["L0_c"], f32).T,
                        np.asarray(inputs["L1_c"], f32).T], axis=1))
    in1 = []
    for c in range(8):
        m = {"ccT": ccT,
             "lutv": np.asarray(inputs["L0_ffn_lut"], f32).reshape(1, 16)}
        wm0 = np.asarray(inputs["L0_wmod"], f32)[:, c * 1536:(c + 1) * 1536]
        wm1 = np.asarray(inputs["L1_wmod"], f32)[:, c * 1536:(c + 1) * 1536]
        m["wmod"] = np.ascontiguousarray(np.concatenate([wm0, wm1], axis=1))
        sel = np.zeros((4, 3072), dtype=f32)
        gcols = np.arange(1536)
        for li in range(2):
            gl = c * 1536 + gcols
            blk = gl // 2048
            sel[:, li * 1536 + gcols] = np.where((blk == 2) | (blk == 5), 1.0, 0.0)
        m["clampsel"] = sel
        m["gi"] = np.ascontiguousarray(
            np.asarray(inputs["L0_ffn_gate_idx"], np.int32).T[:, c * 688:(c + 1) * 688])
        m["ui"] = np.ascontiguousarray(
            np.asarray(inputs["L0_ffn_up_idx"], np.int32).T[:, c * 688:(c + 1) * 688])
        m["di"] = np.ascontiguousarray(
            np.asarray(inputs["L0_ffn_down_idx"], np.int32).T[:, c * 256:(c + 1) * 256])
        m["wg32"] = np.ascontiguousarray(
            np.asarray(inputs["L1_ffn_w_gating"], f32)[:, :, c * 1024:(c + 1) * 1024])
        m["wl32"] = np.ascontiguousarray(
            np.asarray(inputs["L1_ffn_w_linear"], f32)[c * 1024:(c + 1) * 1024, :])
        in1.append(m)
    r1 = run(build_L1, in1)

    mod = np.zeros((2, 4, 6 * D), dtype=f32)
    for c in range(8):
        ms = r1[c]["modsh"]
        mod[0][:, c * 1536:(c + 1) * 1536] = ms[:, 0:1536]
        mod[1][:, c * 1536:(c + 1) * 1536] = ms[:, 1536:3072]
    gwT = np.concatenate([r1[c]["gw"] for c in range(8)], axis=1)
    uwT = np.concatenate([r1[c]["uw"] for c in range(8)], axis=1)
    dwT = np.concatenate([r1[c]["dw"] for c in range(8)], axis=1)
    wg8 = np.concatenate([r1[c]["wg8"] for c in range(8)], axis=2)
    wl8 = np.concatenate([r1[c]["wl8"] for c in range(8)], axis=0)

    def modrow(L, k, b):
        return mod[L][b, k * D:(k + 1) * D]

    perm = build_qkv_perm()
    wq_t = tile_lhsT(np.asarray(inputs["L0_attn_W_qkv"], f32)[:, perm], TF, 48)
    bq = np.asarray(inputs["L0_attn_b_qkv"], f32)[perm]
    bq_cols = np.ascontiguousarray(bq.reshape(48, 128).T)
    inv0 = 1.0 / (10000.0 ** (np.arange(0, 32, 2) / 32.0))
    in2 = []
    for c in range(8):
        b, hf = cores[c]
        sl = slice(hf * TOK, (hf + 1) * TOK)
        tglob = np.arange(hf * TOK, (hf + 1) * TOK, dtype=np.float64)
        ang = np.outer(np.tile(inv0, 8), tglob)
        m = {
            "xfm": np.ascontiguousarray(x0[b, sl].T),
            "wq_t": wq_t, "bq_cols": bq_cols,
            "ca_cols": cols16(modrow(0, 1, b)),
            "sa_cols": cols16(modrow(0, 0, b)),
            "r0c": np.cos(ang).astype(f32),
            "r0s": np.sin(ang).astype(f32),
        }
        in2.append(m)
    r2 = run(build_L2, in2)

    qkv_b = [np.concatenate([r2[2 * b]["qkvp"], r2[2 * b + 1]["qkvp"]], axis=1)
             for b in range(4)]

    def head_rows(base, h, n):
        return slice(base + h * n, base + (h + 1) * n)

    in3 = []
    for c in range(8):
        b, hg = cores[c]
        qk = qkv_b[b]
        qs, ks, vs = [], [], []
        for j in range(8):
            h = hg * 8 + j
            q = np.concatenate([qk[head_rows(0, h, 16)], qk[head_rows(256, h, 16)],
                                qk[head_rows(1024, h, 96)]], axis=0)
            k = np.concatenate([qk[head_rows(512, h, 16)], qk[head_rows(768, h, 16)],
                                qk[head_rows(2560, h, 96)]], axis=0)
            v = qk[head_rows(4096, h, 128)]
            qs.append(q)
            ks.append(k)
            vs.append(np.ascontiguousarray(v.T))
        in3.append({
            "qf": np.ascontiguousarray(np.concatenate(qs, axis=0)),
            "kf": np.ascontiguousarray(np.concatenate(ks, axis=0)),
            "vt": np.ascontiguousarray(np.concatenate(vs, axis=0)),
            "mask4": mask4_np(),
        })
    r3 = run(build_L3, in3)

    o0_b = [np.concatenate([r3[2 * b]["of"], r3[2 * b + 1]["of"]], axis=0)
            for b in range(4)]

    wd_t = tile_lhsT(np.asarray(inputs["L0_attn_W_dense"], f32), TF, TF)
    bd_cols = cols16(np.asarray(inputs["L0_attn_b_dense"], f32))
    gwt = tile_lhsT(gwT, TF, NT0)
    uwt = tile_lhsT(uwT, TF, NT0)
    dwt = tile_lhsT(dwT, NT0, TF)
    wqT = np.ascontiguousarray(np.asarray(inputs["L1_attn_Wq"], f32).T)
    wkvT = np.asarray(inputs["L1_attn_Wkv"], f32).T
    wkvhT = np.asarray(inputs["L1_attn_Wkvh"], f32).T
    wgT = np.asarray(inputs["L1_attn_Wg"], f32).T
    kvhg_pad = np.zeros((D, 128), dtype=f32)
    kvhg_pad[:, 0:16] = wkvhT[:, 0:16]
    kvhg_pad[:, 32:48] = wkvhT[:, 16:32]
    kvhg_pad[:, 64:80] = wgT[:, 0:16]
    kvhg_pad[:, 96:112] = wgT[:, 16:32]
    wl1 = np.concatenate([wqT, wkvT[:, 0:128], wkvT[:, 128:256], kvhg_pad], axis=1)
    wl1_t = tile_lhsT(np.ascontiguousarray(wl1), TF, 19)
    inv1 = 10000.0 ** (-np.arange(0, 128, 2) / 128.0)

    in4 = []
    for c in range(8):
        b, hf = cores[c]
        sl = slice(hf * TOK, (hf + 1) * TOK)
        tglob = np.arange(hf * TOK, (hf + 1) * TOK, dtype=np.float64)
        ang = np.outer(np.concatenate([inv1, inv1]), tglob)
        m = {
            "xfm": np.ascontiguousarray(x0[b, sl].T),
            "ofm": np.ascontiguousarray(o0_b[b][:, sl]),
            "wd_t": wd_t, "bd_cols": bd_cols,
            "ga_cols": cols16(modrow(0, 2, b)),
            "cb_cols": cols16(modrow(0, 4, b)),
            "sbv_cols": cols16(modrow(0, 3, b)),
            "gb_cols": cols16(modrow(0, 5, b)),
            "ca1_cols": cols16(modrow(1, 1, b)),
            "sa1_cols": cols16(modrow(1, 0, b)),
            "gwt": gwt, "uwt": uwt, "dwt": dwt, "wl1_t": wl1_t,
            "r1c": np.cos(ang).astype(f32),
            "r1s": np.sin(ang).astype(f32),
        }
        in4.append(m)
    r4 = run(build_L4, in4)

    x2_b = [np.concatenate([r4[2 * b]["x2fm"], r4[2 * b + 1]["x2fm"]], axis=1)
            for b in range(4)]
    q1_b = [np.concatenate([r4[2 * b]["q1p"], r4[2 * b + 1]["q1p"]], axis=1)
            for b in range(4)]
    kp_b = [np.concatenate([r4[2 * b]["kp_r"], r4[2 * b + 1]["kp_r"]], axis=1)
            for b in range(4)]
    vp_b = [np.concatenate([r4[2 * b]["vp_f"], r4[2 * b + 1]["vp_f"]], axis=1)
            for b in range(4)]
    kvhg_b = [np.concatenate([r4[2 * b]["kvhg"], r4[2 * b + 1]["kvhg"]], axis=1)
              for b in range(4)]

    identf = np.eye(128, dtype=f32)
    in5 = []
    for c in range(8):
        b, hg = cores[c]
        ordh = list(range(hg * 8, 16)) + list(range(0, hg * 8))
        kv = np.zeros((128, S), dtype=f32)
        src = kvhg_b[b]
        kv[0:16] = src[0:16][ordh[:8] + ordh[8:]][:, :] if False else src[0:16][ordh]
        kv[32:48] = src[32:48][ordh]
        kv[64:80] = src[64:80][ordh]
        kv[96:112] = src[96:112][ordh]
        ehrow = np.zeros((16, 8 * 128), dtype=f32)
        for j in range(8):
            ehrow[j, j * 128:(j + 1) * 128] = 1.0
        qs = [q1_b[b][(hg * 8 + j) * 128:(hg * 8 + j + 1) * 128] for j in range(8)]
        in5.append({
            "q1": np.ascontiguousarray(np.concatenate(qs, axis=0)),
            "kpr": np.ascontiguousarray(kp_b[b]),
            "vpt": np.ascontiguousarray(vp_b[b].T),
            "kvhg": kv,
            "mask4": mask4_np(),
            "ehrow": ehrow,
            "identf": identf,
        })
    r5 = run(build_L5, in5)

    o1_b = []
    for b in range(4):
        o = np.zeros((2048, 2048), dtype=BF)
        for hg in range(2):
            o[hg * 1024:(hg + 1) * 1024] = r5[2 * b + hg]["o1"]
        o1_b.append(o)

    wo_t = tile_lhsT(np.ascontiguousarray(np.asarray(inputs["L1_attn_Wout"], f32).T), TF, TF)
    w8g_t = np.concatenate([tile_lhsT(wg8[0], TF, NT1), tile_lhsT(wg8[1], TF, NT1)], axis=0)
    wl8_t = tile_lhsT(wl8, NT1, TF)
    in6 = []
    for c in range(8):
        b, hf = cores[c]
        sl = slice(hf * TOK, (hf + 1) * TOK)
        in6.append({
            "xfm": np.ascontiguousarray(x2_b[b][:, sl]),
            "ofm": np.ascontiguousarray(o1_b[b][:, sl]),
            "wo_t": wo_t,
            "ga_cols": cols16(modrow(1, 2, b)),
            "cb_cols": cols16(modrow(1, 4, b)),
            "sbv_cols": cols16(modrow(1, 3, b)),
            "gb_cols": cols16(modrow(1, 5, b)),
            "w8g_t": w8g_t, "wl8_t": wl8_t,
            "identf": identf,
        })
    r6 = run(build_L6, in6)

    out = np.zeros((B, S, D), dtype=f32)
    for c in range(8):
        b, hf = cores[c]
        out[b, hf * TOK:(hf + 1) * TOK] = r6[c]["xout"]
    return out
